# version 10
# baseline (speedup 1.0000x reference)
import numpy as np
from contextlib import ExitStack

import concourse.bass as bass
import concourse.mybir as mybir
from concourse.bass_utils import run_bass_kernel_spmd

F32 = mybir.dt.float32
BF16 = mybir.dt.bfloat16
AF = mybir.ActivationFunctionType
ALU = mybir.AluOpType
AX = mybir.AxisListType

NCORE = 8
T = 2048
NT = 16
NJ = 4
D = 1024
H3 = 3072
NE = 8
FF = 4096
G = 512
NG = FF // G
EPS = 1e-5
MAGIC = float(np.int32(0x5F3759DF).view(np.float32))


def build(mode='full', nt_lim=NT):
    nc = bass.Bass("TRN2", target_bir_lowering=False)
    x_d = nc.dram_tensor("x", [128, NT, D], F32, kind="ExternalInput").ap()
    wqkv_d = nc.dram_tensor("wqkv", [D, H3], F32, kind="ExternalInput").ap()
    bqkv_d = nc.dram_tensor("bqkv", [H3], F32, kind="ExternalInput").ap()
    wout_d = nc.dram_tensor("wout", [D, D], F32, kind="ExternalInput").ap()
    bout_d = nc.dram_tensor("bout", [D], F32, kind="ExternalInput").ap()
    wg_d = nc.dram_tensor("wg", [NE, D, FF], F32, kind="ExternalInput").ap()
    wu_d = nc.dram_tensor("wu", [NE, D, FF], F32, kind="ExternalInput").ap()
    wd_d = nc.dram_tensor("wd", [NE, FF, D], F32, kind="ExternalInput").ap()
    out_d = nc.dram_tensor("out", [128, NT, D], F32, kind="ExternalOutput").ap()
    xsp_d = nc.dram_tensor("xspill", [128, NT, D], F32, kind="Internal").ap()

    ctx = ExitStack()
    S = lambda n, sh, dt: ctx.enter_context(nc.sbuf_tensor(n, sh, dt))

    ident = S("ident", [128, 128], BF16)
    ones1 = S("ones1", [1, 128], BF16)
    consts = S("consts", [128, 3], F32)
    bq_sb = S("bq_sb", [1, H3], BF16)
    bo_sb = S("bo_sb", [1, D], BF16)
    ln2T = S("ln2T", [128, 8, T], BF16)
    stats = S("stats", [128, 2, 8], F32)
    scores = S("scores", [128, 256], F32)
    mx = S("mx", [128, 16], F32)
    sc2 = S("sc2", [128, 256], BF16)
    probs = S("probs", [128, 256], BF16)
    probsn = S("probsn", [128, 256], BF16)
    ssum = S("ssum", [128, 16], F32)
    rs = S("rs", [128, 16], F32)

    steps = {k: [] for k in "gvtsy"}

    def em(k, fn):
        steps[k].append(fn)

    class Ch:
        __slots__ = ("h", "v")

        def __init__(self, name):
            self.h = ctx.enter_context(nc.semaphore(name))
            self.v = 0

        def wt(self, k, tok=None):
            t_ = self.v if tok is None else tok
            em(k, lambda e, h=self.h, t=t_: e.wait_ge(h, t))

    def op(k, fn, ch=None, n=1):
        if ch is not None:
            ch.v += n
            em(k, lambda e, fn=fn, h=ch.h, n=n: fn(e).then_inc(h, n))
            return ch.v
        em(k, fn)
        return None

    XLa = Ch("XLa"); XLb = Ch("XLb")
    STa = Ch("STa"); STb = Ch("STb")
    STD = Ch("STD")
    XCa = Ch("XCa"); XCb = Ch("XCb")
    SP = Ch("SP")
    STp = [None, None]
    WIq = Ch("WIq"); WIo = Ch("WIo")
    WLa = Ch("WLa"); WLb = Ch("WLb")
    GI = Ch("GI")
    VV = Ch("VV")
    VE = Ch("VE")
    MUL = Ch("MUL"); ACC = Ch("ACC"); LNA = Ch("LNA")
    SS = Ch("SS")
    QE = Ch("QE")
    SIL = Ch("SIL")
    CS = Ch("CS")
    GP = Ch("GP")
    TB = Ch("TB")
    MM = Ch("MM")
    MMG = Ch("MMG"); MMU = Ch("MMU"); MMD = Ch("MMD")

    acc_ctx = ExitStack()
    actx = ExitStack()
    Sa = lambda n, sh, dt: actx.enter_context(nc.sbuf_tensor(n, sh, dt))
    Pa = lambda n, sh, dt: actx.enter_context(nc.psum_tensor(n, sh, dt))
    xf = Sa("xf", [128, 2, 4, D], F32)
    wqkv_sb = Sa("wqkv_sb", [128, 8, H3], BF16)
    wout_sb = Sa("wout_sb", [128, 8, D], BF16)
    ln1 = Sa("ln1", [128, 2, D], BF16)
    ln1T = Sa("ln1T", [128, 2, 8, 128], BF16)
    qkv = Sa("qkv", [128, 2, H3], BF16)
    prod = Sa("prod", [128, 2, 4096], BF16)
    ctxt = Sa("ctxt", [128, 2, D], BF16)
    ctxT = Sa("ctxT", [128, 2, 8, 128], BF16)
    ln2tok = Sa("ln2tok", [128, 2, D], BF16)
    scr = Sa("scr", [128, 2, D], F32)
    vtc = Sa("vtc", [128, 1024], BF16)
    tmpl = Sa("tmpl", [128, D], F32)
    ps_tr = Pa("ps_tr", [128, 2, 8, 128], BF16)
    ps_mm = Pa("ps_mm", [128, 2, 512], F32)
    ps_att = Pa("ps_att", [128, 2, 2, 512], F32)

    x_r = x_d.rearrange("p (j i) d -> p j i d", i=4)
    xsp_r = xsp_d.rearrange("p (j i) d -> p j i d", i=4)
    op('y', lambda e: e.dma_start(xf[:, 0], x_r[:, 0]), XLa, 16)
    op('y', lambda e: e.dma_start(xf[:, 1], x_r[:, 1]), XLb, 16)
    op('g', lambda g: g.dma_start(wqkv_sb[:], wqkv_d.rearrange("(k p) e -> p k e", p=128)), WIq, 16)
    op('g', lambda g: g.dma_start(bq_sb[:], bqkv_d[None, :]), WIq, 16)
    op('g', lambda g: g.dma_start(wout_sb[:], wout_d.rearrange("(k p) e -> p k e", p=128)), WIo, 16)
    op('g', lambda g: g.dma_start(bo_sb[:], bout_d[None, :]), WIo, 16)
    op('g', lambda g: g.memset(ident[:], 0.0), GI)
    GI.wt('g')
    op('g', lambda g: g.affine_select(out=ident[:], in_=ident[:], compare_op=ALU.not_equal,
                                      fill=1.0, base=0, pattern=[[-1, 128]], channel_multiplier=1), GI)
    op('v', lambda v: v.memset(ones1[:], 1.0))
    op('v', lambda v: v.memset(consts[:, 0:1], EPS))
    op('v', lambda v: v.memset(consts[:, 1:2], -0.5))
    op('v', lambda v: v.memset(consts[:, 2:3], MAGIC), VV)

    def ln_stats(src_fn, par):
        st_ = stats[:, par]
        op('v', lambda v: v.tensor_mul(scr[:, par], src_fn(), src_fn()))
        op('v', lambda v: v.reduce_sum(st_[:, 0:1], src_fn(), axis=AX.X), VV)
        VV.wt('v')
        op('v', lambda v: v.reduce_sum(st_[:, 1:2], scr[:, par], axis=AX.X))
        op('v', lambda v: v.tensor_scalar_mul(st_[:, 0:1], st_[:, 0:1], 1.0 / D), VV)
        VV.wt('v')
        op('v', lambda v: v.tensor_scalar_mul(st_[:, 1:2], st_[:, 1:2], 1.0 / D))
        op('v', lambda v: v.tensor_mul(st_[:, 2:3], st_[:, 0:1], st_[:, 0:1]), VV)
        VV.wt('v')
        op('v', lambda v: v.tensor_sub(st_[:, 2:3], st_[:, 1:2], st_[:, 2:3]), VV)
        VV.wt('s')
        op('s', lambda s: s.activation(st_[:, 3:4], st_[:, 2:3], AF.Ln, bias=consts[:, 0:1], scale=1.0), SS)
        SS.wt('s')
        op('s', lambda s: s.activation(st_[:, 4:5], st_[:, 3:4], AF.Exp, scale=consts[:, 1:2]), SS)
        SS.wt('v')
        return op('v', lambda v: v.tensor_scalar(st_[:, 5:6], st_[:, 0:1], st_[:, 4:5], -1.0,
                                                 ALU.mult, ALU.mult), VV)

    def ln_stats_psum(src_fn, par):
        st_ = stats[:, par]
        op('v', lambda v: v.reduce_sum(st_[:, 0:1], src_fn(), axis=AX.X), VV)
        VV.wt('s')
        op('s', lambda s: s.activation(scr[:, par], src_fn(), AF.Square), SS)
        SS.wt('v')
        op('v', lambda v: v.reduce_sum(st_[:, 1:2], scr[:, par], axis=AX.X))
        op('v', lambda v: v.tensor_scalar_mul(st_[:, 0:1], st_[:, 0:1], 1.0 / D), VV)
        VV.wt('v')
        op('v', lambda v: v.tensor_scalar_mul(st_[:, 1:2], st_[:, 1:2], 1.0 / D))
        op('v', lambda v: v.tensor_mul(st_[:, 2:3], st_[:, 0:1], st_[:, 0:1]), VV)
        VV.wt('v')
        op('v', lambda v: v.tensor_sub(st_[:, 2:3], st_[:, 1:2], st_[:, 2:3]), VV)
        VV.wt('s')
        op('s', lambda s: s.activation(st_[:, 3:4], st_[:, 2:3], AF.Ln, bias=consts[:, 0:1], scale=1.0), SS)
        SS.wt('s')
        op('s', lambda s: s.activation(st_[:, 4:5], st_[:, 3:4], AF.Exp, scale=consts[:, 1:2]), SS)
        SS.wt('v')
        return op('v', lambda v: v.tensor_scalar(st_[:, 5:6], st_[:, 0:1], st_[:, 4:5], -1.0,
                                                 ALU.mult, ALU.mult), VV)

    tok_tr_evac = {}
    tok_qkv_mm = {}
    tok_wout_mm = {}
    tok_qkv_evac = {}
    tok_hm_done = {}
    tok_ln1_tr = {}
    tok_ctx_tr = {}
    tok_lna = {}
    tok_ln2_ev = {}
    tok_prod = {}
    tok_ln1_w = {}
    tok_ln1T_ev = {}
    tok_qkv_ev_last = {}
    n_trb = [0]
    n_qg = [0]
    n_hg = [0]

    def transpose_batch(src_fn, dst):
        c = n_trb[0]; n_trb[0] += 1
        s_ = c % 2
        if c - 2 in tok_tr_evac:
            VE.wt('t', tok_tr_evac[c - 2])
        tr_tok = None
        for k in range(8):
            tr_tok = op('t', lambda t, k=k, s_=s_: t.transpose(ps_tr[:, s_, k], src_fn(k), ident[:]),
                        TB if k == 7 else None)
        TB.wt('v', tr_tok)
        ev = op('v', lambda v, s_=s_, dst=dst: v.tensor_copy(dst, ps_tr[:, s_]), VE)
        tok_tr_evac[c] = ev
        return tr_tok, ev

    qh = qkv
    ph = probsn

    def S1(t_i):
        j, i = divmod(t_i, 4)
        jp = j % 2
        p2 = t_i % 2
        xt = xf[:, jp, i]
        if t_i == 0:
            GI.wt('v'); VV.wt('v')
        if i == 0:
            XJ = XLa if j % 2 == 0 else XLb
            XJ.wt('v', 16 * (j // 2 + 1))
        ln_stats(lambda xt=xt: xt, p2)
        VV.wt('v')
        if t_i >= 2:
            TB.wt('v', tok_ln1_tr[t_i - 2])
        tok_ln1_w[t_i] = op('v', lambda v, xt=xt, p2=p2: v.tensor_scalar(
            ln1[:, p2], xt, stats[:, p2, 4:5], stats[:, p2, 5:6], ALU.mult, ALU.add), VE)

    def S2(t_i):
        p2 = t_i % 2
        VE.wt('t', tok_ln1_w[t_i])
        if t_i == 0:
            GI.wt('t')
        if t_i >= 2:
            MM.wt('v', tok_qkv_mm[t_i - 2])
        trt, ev1 = transpose_batch(lambda k, p2=p2: ln1[:, p2, k * 128:(k + 1) * 128], ln1T[:, p2])
        tok_ln1_tr[t_i] = trt
        tok_ln1T_ev[t_i] = ev1

    def S3(t_i):
        p2 = t_i % 2
        VE.wt('t', tok_ln1T_ev[t_i])
        if t_i == 0:
            WIq.wt('t')
        if t_i >= 2:
            VV.wt('s', tok_hm_done[t_i - 2])
        for n in range(6):
            b = n % 2
            if n_qg[0] - 2 in tok_qkv_evac:
                QE.wt('t', tok_qkv_evac[n_qg[0] - 2])
            op('t', lambda t, n=n, b=b: t.matmul(ps_mm[:, b], ones1[:],
                                                 bq_sb[:, n * 512:(n + 1) * 512], start=True, stop=False))
            mt = None
            for k in range(8):
                mt = op('t', lambda t, n=n, b=b, k=k, p2=p2: t.matmul(
                    ps_mm[:, b], ln1T[:, p2, k], wqkv_sb[:, k, n * 512:(n + 1) * 512],
                    start=False, stop=(k == 7)), MM if k == 7 else None)
            MM.wt('s', mt)
            tok_qkv_evac[n_qg[0]] = op(
                's', lambda s, n=n, b=b, p2=p2: s.activation(
                    qkv[:, p2, n * 512:(n + 1) * 512], ps_mm[:, b], AF.Copy), QE)
            n_qg[0] += 1
        tok_qkv_mm[t_i] = MM.v
        tok_qkv_ev_last[t_i] = QE.v

    def S4(t_i):
        p2 = t_i % 2
        QE.wt('v', tok_qkv_ev_last[t_i])
        qb = 3072 * p2
        op('v', lambda v, qb=qb: v.tensor_copy(
            vtc[:].rearrange("p (d g) -> p d g", g=16),
            bass.AP(qh, qb + 128, [[2 * H3, 128], [1, 64], [192, 16]])), VV)
        for grp in range(4):
            h0 = grp * 4
            gg = n_hg[0]; n_hg[0] += 1
            sl = gg % 2
            if gg - 2 in tok_prod:
                VV.wt('v', tok_prod[gg - 2])
            qap = bass.AP(qh, qb + 192 * h0, [[2 * H3, 128], [192, 4], [0, 16], [1, 64]])
            kap = bass.AP(qh, qb + 64, [[2 * H3, 128], [0, 4], [192, 16], [1, 64]])
            op('v', lambda v, sl=sl, qap=qap, kap=kap: v.tensor_tensor(
                prod[:, sl].rearrange("p (h g d) -> p h g d", h=4, g=16), qap, kap, ALU.mult), VV)
            VV.wt('v')
            for half in (32, 16, 8):
                op('v', lambda v, sl=sl, half=half: v.tensor_add(
                    bass.AP(prod, sl * 4096, [[8192, 128], [1024, 4], [64, 16], [1, half]]),
                    bass.AP(prod, sl * 4096, [[8192, 128], [1024, 4], [64, 16], [1, half]]),
                    bass.AP(prod, sl * 4096 + half, [[8192, 128], [1024, 4], [64, 16], [1, half]])), VV)
                VV.wt('v')
            tok_prod[gg] = op('v', lambda v, grp=grp, sl=sl: v.reduce_sum(
                scores[:, grp * 64:(grp + 1) * 64].rearrange("p (h g) -> p h g", h=4),
                bass.AP(prod, sl * 4096, [[8192, 128], [1024, 4], [64, 16], [1, 8]]),
                axis=AX.X), VV)
        VV.wt('v')
        op('v', lambda v: v.tensor_scalar_mul(scores[:], scores[:], 0.125), VV)
        VV.wt('v')
        op('v', lambda v: v.reduce_max(mx[:], scores[:].rearrange("p (h g) -> p h g", h=16), axis=AX.X), VV)
        VV.wt('v')
        op('v', lambda v: v.tensor_sub(sc2[:].rearrange("p (h g) -> p h g", h=16),
                                       scores[:].rearrange("p (h g) -> p h g", h=16),
                                       mx[:].unsqueeze(2).broadcast_to([128, 16, 16])), VV)
        VV.wt('s')
        op('s', lambda s: s.activation(probs[:], sc2[:], AF.Exp), SS)
        SS.wt('v')
        op('v', lambda v: v.reduce_sum(ssum[:], probs[:].rearrange("p (h g) -> p h g", h=16), axis=AX.X), VV)
        VV.wt('v')
        op('v', lambda v: v.reciprocal(rs[:], ssum[:]), VV)
        VV.wt('v')
        op('v', lambda v: v.tensor_tensor(probsn[:].rearrange("p (h g) -> p h g", h=16),
                                          probs[:].rearrange("p (h g) -> p h g", h=16),
                                          rs[:].unsqueeze(2).broadcast_to([128, 16, 16]), ALU.mult), VV)
        VV.wt('v')
        if t_i >= 2:
            TB.wt('v', tok_ctx_tr[t_i - 2])
        for grp in range(4):
            h0 = grp * 4
            gg = n_hg[0]; n_hg[0] += 1
            sl = gg % 2
            if gg - 2 in tok_prod:
                VV.wt('v', tok_prod[gg - 2])
            pap = bass.AP(ph, 16 * h0, [[256, 128], [16, 4], [0, 64], [1, 16]])
            vap = bass.AP(vtc, 0, [[1024, 128], [0, 4], [16, 64], [1, 16]])
            op('v', lambda v, sl=sl, pap=pap, vap=vap: v.tensor_tensor(
                prod[:, sl].rearrange("p (h d g) -> p h d g", h=4, d=64), pap, vap, ALU.mult), VV)
            VV.wt('v')
            for half in (8, 4, 2):
                op('v', lambda v, sl=sl, half=half: v.tensor_add(
                    bass.AP(prod, sl * 4096, [[8192, 128], [1024, 4], [16, 64], [1, half]]),
                    bass.AP(prod, sl * 4096, [[8192, 128], [1024, 4], [16, 64], [1, half]]),
                    bass.AP(prod, sl * 4096 + half, [[8192, 128], [1024, 4], [16, 64], [1, half]])), VV)
                VV.wt('v')

            def cxred(v, grp=grp, sl=sl, p2=p2):
                with nc.allow_low_precision(reason="context feeds a bf16 matmul"):
                    return v.tensor_add(
                        ctxt[:, p2, grp * 256:(grp + 1) * 256].rearrange("p (h d) -> p h d", h=4),
                        bass.AP(prod, sl * 4096, [[8192, 128], [1024, 4], [16, 64], [1, 1]]),
                        bass.AP(prod, sl * 4096 + 1, [[8192, 128], [1024, 4], [16, 64], [1, 1]]))
            tok_prod[gg] = op('v', cxred, VV)
        tok_hm_done[t_i] = VV.v

    def S5(t_i):
        p2 = t_i % 2
        VV.wt('t', tok_hm_done[t_i])
        if t_i >= 2:
            MM.wt('v', tok_wout_mm[t_i - 2])
        trt2, ev2 = transpose_batch(lambda k, p2=p2: ctxt[:, p2, k * 128:(k + 1) * 128], ctxT[:, p2])
        tok_ctx_tr[t_i] = trt2
        VE.wt('t', ev2)
        if t_i == 0:
            WIo.wt('t')
        if t_i >= 2:
            LNA.wt('t', tok_lna[t_i - 2])
        mt = None
        for n in range(2):
            op('t', lambda t, n=n, p2=p2: t.matmul(ps_att[:, p2, n], ones1[:],
                                                   bo_sb[:, n * 512:(n + 1) * 512], start=True, stop=False))
            for k in range(8):
                mt = op('t', lambda t, n=n, k=k, p2=p2: t.matmul(
                    ps_att[:, p2, n], ctxT[:, p2, k], wout_sb[:, k, n * 512:(n + 1) * 512],
                    start=False, stop=(k == 7)), MM if k == 7 else None)
        tok_wout_mm[t_i] = mt

    def S6(t_i):
        j, i = divmod(t_i, 4)
        jp = j % 2
        p2 = t_i % 2
        xt = xf[:, jp, i]
        mt = tok_wout_mm[t_i]
        MM.wt('v', mt)
        MM.wt('s', mt)
        psa = lambda p2=p2: ps_att[:, p2].rearrange("p a b -> p (a b)")
        ln_stats_psum(psa, p2)
        VV.wt('v')
        op('v', lambda v, p2=p2, psa=psa: v.tensor_scalar(
            tmpl[:], psa(), stats[:, p2, 4:5], stats[:, p2, 5:6], ALU.mult, ALU.add), LNA)
        tok_lna[t_i] = LNA.v
        LNA.wt('v', tok_lna[t_i])
        op('v', lambda v, xt=xt: v.tensor_add(xt, xt, tmpl[:]), VV)
        VV.wt('v')

    def S7(t_i):
        j, i = divmod(t_i, 4)
        jp = j % 2
        p2 = t_i % 2
        xt = xf[:, jp, i]
        ln_stats(lambda xt=xt: xt, p2)
        VV.wt('v')
        if t_i >= 2:
            VE.wt('v', tok_ln2_ev[t_i - 2])
        w_ln2 = op('v', lambda v, xt=xt, p2=p2: v.tensor_scalar(
            ln2tok[:, p2], xt, stats[:, p2, 4:5], stats[:, p2, 5:6], ALU.mult, ALU.add), VE)
        VE.wt('t', w_ln2)
        _, ev = transpose_batch(lambda k, p2=p2: ln2tok[:, p2, k * 128:(k + 1) * 128],
                                ln2T[:, :, t_i * 128:(t_i + 1) * 128])
        tok_ln2_ev[t_i] = ev
        if i == 3:
            VE.wt('y', ev)
            op('y', lambda e, jp=jp, j=j: e.dma_start(xsp_r[:, j], xf[:, jp]), SP, 16)
            if j + 2 < NJ:
                SP.wt('y')
                op('y', lambda e, j=j, jp=jp: e.dma_start(xf[:, jp], x_r[:, j + 2]),
                   XLa if (j + 2) % 2 == 0 else XLb, 16)

    S1(0)
    for t_i in range(nt_lim):
        if t_i >= 1:
            S6(t_i - 1)
            S7(t_i - 1)
        S2(t_i)
        S3(t_i)
        if t_i + 1 < nt_lim:
            S1(t_i + 1)
        S4(t_i)
        S5(t_i)
    S6(nt_lim - 1)
    S7(nt_lim - 1)

    AT_tok = TB.v
    AV_tok = VE.v
    AS_tok = SS.v
    actx.close()

    if mode == 'A':
        SP.wt('y')
        op('y', lambda e: e.dma_start(out_d, xsp_d), STD, 16)
        STD.wt('y')
        with nc.Block() as block:
            block.gpsimd(lambda g: [f(g) for f in steps['g']])
            block.vector(lambda v: [f(v) for f in steps['v']])
            block.tensor(lambda t: [f(t) for f in steps['t']])
            block.scalar(lambda s: [f(s) for f in steps['s']])
            block.sync(lambda sy: [f(sy) for f in steps['y']])
        ctx.close()
        return nc

    acc = acc_ctx.enter_context(nc.sbuf_tensor("acc", [128, 8, T], F32))
    acc_bf = acc_ctx.enter_context(nc.sbuf_tensor("acc_bf", [128, 8, T], BF16))
    bctx = ExitStack()
    Sb = lambda n, sh, dt: bctx.enter_context(nc.sbuf_tensor(n, sh, dt))
    Pb = lambda n, sh, dt: bctx.enter_context(nc.psum_tensor(n, sh, dt))
    wg_b = Sb("wg_b", [128, 2, 8, G], BF16)
    wu_b = Sb("wu_b", [128, 2, 8, G], BF16)
    wd_b = Sb("wd_b", [128, 2, 4, D], BF16)
    hbuf = Sb("hbuf", [128, 2, 4, 512], BF16)
    silb = Sb("silb", [128, 2, 512], BF16)
    ps_g = Pb("ps_g", [128, 2, 512], F32)
    ps_u = Pb("ps_u", [128, 2, 512], F32)
    ps_d = Pb("ps_d", [128, 4, 512], F32)


    WL = [WLa, WLb]
    wl_cnt = [0, 0]
    mmd_gp = {}
    silu_tok = {}
    mul_tok = {}
    accw_tok = {}
    spill_done = 16 * NJ
    cglob = [0]

    NQ = NE * NG * NJ
    mmd_q = {}

    def emit_weights(gp):
        par = gp % 2
        if gp == 0:
            TB.wt('g', AT_tok); VE.wt('g', AV_tok); SS.wt('g', AS_tok)
        if gp >= 2:
            MMD.wt('g', mmd_gp[gp - 2])
        e, gi = divmod(gp, NG)
        op('g', lambda g, e=e, gi=gi, par=par: g.dma_start(
            wg_b[:, par], wg_d[e].rearrange("(k p) f -> p k f", p=128)[:, :, gi * G:(gi + 1) * G]),
            WL[par], 16)
        op('g', lambda g, e=e, gi=gi, par=par: g.dma_start(
            wu_b[:, par], wu_d[e].rearrange("(k p) f -> p k f", p=128)[:, :, gi * G:(gi + 1) * G]),
            WL[par], 16)
        op('g', lambda g, e=e, gi=gi, par=par: g.dma_start(
            wd_b[:, par], wd_d[e].rearrange("(k p) f -> p k f", p=128)[:, gi * 4:(gi + 1) * 4, :]),
            WL[par], 16)
        wl_cnt[par] += 48

    def emit_P1(q):
        gp, jj = divmod(q, NJ)
        par = gp % 2
        toks = slice(jj * 512, (jj + 1) * 512)
        if jj == 0:
            emit_weights(gp)
            WL[par].wt('t', wl_cnt[par])
        for kh in range(4):
            c = cglob[0]; cglob[0] += 1
            bk = c % 2
            if c - 2 in silu_tok:
                SIL.wt('t', silu_tok[c - 2])
                MUL.wt('t', mul_tok[c - 2])
            if kh == 0 and q - 2 in mmd_q:
                MMD.wt('v', mmd_q[q - 2])
            gt = None
            for kd in range(8):
                gt = op('t', lambda t, par=par, kh=kh, kd=kd, bk=bk, toks=toks: t.matmul(
                    ps_g[:, bk], wg_b[:, par, kd, kh * 128:(kh + 1) * 128], ln2T[:, kd, toks],
                    start=(kd == 0), stop=(kd == 7)), MMG if kd == 7 else None)
            ut = None
            for kd in range(8):
                ut = op('t', lambda t, par=par, kh=kh, kd=kd, bk=bk, toks=toks: t.matmul(
                    ps_u[:, bk], wu_b[:, par, kd, kh * 128:(kh + 1) * 128], ln2T[:, kd, toks],
                    start=(kd == 0), stop=(kd == 7)), MMU if kd == 7 else None)
            MMG.wt('s', gt)
            if c - 2 in mul_tok:
                MUL.wt('s', mul_tok[c - 2])
            silu_tok[c] = op('s', lambda s, bk=bk, c=c: s.activation(
                silb[:, c % 2], ps_g[:, bk], AF.Silu), SIL)
            MMU.wt('v', ut)
            SIL.wt('v', silu_tok[c])
            mul_tok[c] = op('v', lambda v, kh=kh, bk=bk, c=c, q=q: v.tensor_tensor(
                hbuf[:, q % 2, kh], silb[:, c % 2], ps_u[:, bk], ALU.mult), MUL)

    def emit_P2(q):
        gp, jj = divmod(q, NJ)
        par = gp % 2
        toks = slice(jj * 512, (jj + 1) * 512)
        MUL.wt('t', mul_tok[q * 4 + 3])
        for half in range(2):
            for mi in range(4):
                m = half * 4 + mi
                pk = (q, 0, mi) if half == 1 else (q - 1, 1, mi)
                if pk in accw_tok:
                    ACC.wt('t', accw_tok[pk])
                dt_ = None
                for kh in range(4):
                    dt_ = op('t', lambda t, par=par, m=m, mi=mi, kh=kh, q=q: t.matmul(
                        ps_d[:, mi], wd_b[:, par, kh, m * 128:(m + 1) * 128], hbuf[:, q % 2, kh],
                        start=(kh == 0), stop=(kh == 3)), MMD if kh == 3 else None)
                MMD.wt('v', dt_)
                if gp == 0:
                    if q == 0 and half == 0 and mi == 0:
                        SP.wt('v', spill_done)
                    tok = op('v', lambda v, m=m, mi=mi, toks=toks: v.tensor_copy(
                        acc[:, m, toks], ps_d[:, mi]), ACC)
                else:
                    tok = op('v', lambda v, m=m, mi=mi, toks=toks: v.tensor_add(
                        acc[:, m, toks], ps_d[:, mi], acc[:, m, toks]), ACC)
                accw_tok[(q, half, mi)] = tok
        mmd_q[q] = MMD.v
        if jj == NJ - 1:
            mmd_gp[gp] = MMD.v

    for q in range(NQ):
        emit_P1(q)
        if q >= 1:
            emit_P2(q - 1)
    emit_P2(NQ - 1)

    B_acc_done = ACC.v
    bctx.close()

    if mode == 'AB':
        ACC.wt('g')
        op('g', lambda g: g.dma_start(out_d.rearrange("(k p) tt -> p k tt", p=128), acc[:]), STD, 16)
        STD.wt('g')
        with nc.Block() as block:
            block.gpsimd(lambda g: [f(g) for f in steps['g']])
            block.vector(lambda v: [f(v) for f in steps['v']])
            block.tensor(lambda t: [f(t) for f in steps['t']])
            block.scalar(lambda s: [f(s) for f in steps['s']])
            block.sync(lambda sy: [f(sy) for f in steps['y']])
        ctx.close()
        acc_ctx.close()
        return nc

    c2 = ExitStack()
    Sc = lambda n, sh, dt: c2.enter_context(nc.sbuf_tensor(n, sh, dt))
    Pc = lambda n, sh, dt: c2.enter_context(nc.psum_tensor(n, sh, dt))
    acctok = Sc("acctok", [128, 2, D], BF16)
    xrel = Sc("xrel", [128, 2, 4, D], F32)
    tmpc = Sc("tmpc", [128, 2, D], F32)
    scrc = Sc("scrc", [128, 2, D], F32)
    outb = Sc("outb", [128, 2, D], F32)
    ps_c = Pc("ps_c", [128, 2, 8, 128], BF16)

    conv_tok = {}
    for t_c in range(NT):
        g_jj = t_c // 4
        if t_c % 4 == 0:
            ACC.wt('s', accw_tok[((NE * NG - 1) * NJ + g_jj, 1, 3)])
        conv_tok[t_c] = op('s', lambda s, t_c=t_c: s.activation(
            acc_bf[:, :, t_c * 128:(t_c + 1) * 128],
            acc[:, :, t_c * 128:(t_c + 1) * 128], AF.Copy), CS)

    xsp_t = xsp_d
    out_t = out_d
    tok_c_ev = {}
    tok_c_st = {}
    MMD.wt('y'); WLa.wt('y'); WLb.wt('y')
    for g_jj in range(2):
        op('y', lambda e, g_jj=g_jj: e.dma_start(
            xrel[:, g_jj % 2], xsp_r[:, g_jj]), XCa if g_jj % 2 == 0 else XCb, 16)
    tok_c_add = {}
    for t_i in range(NT):
        p2 = t_i % 2
        if t_i == 0:
            ACC.wt('t', B_acc_done)
        CS.wt('t', conv_tok[t_i])
        if t_i >= 2:
            VE.wt('t', tok_c_ev[t_i - 2])
        ct = None
        for k in range(8):
            ct = op('t', lambda t, k=k, t_i=t_i, p2=p2: t.transpose(
                ps_c[:, p2, k], acc_bf[:, k, t_i * 128:(t_i + 1) * 128], ident[:]),
                TB if k == 7 else None)
        TB.wt('v', ct)
        tok_c_ev[t_i] = op('v', lambda v, p2=p2: v.tensor_copy(acctok[:, p2], ps_c[:, p2]), VE)
        VE.wt('v', tok_c_ev[t_i])
        st_ = stats[:, p2]
        op('v', lambda v, p2=p2: v.tensor_mul(scrc[:, p2], acctok[:, p2], acctok[:, p2]))
        op('v', lambda v, p2=p2, st_=st_: v.reduce_sum(st_[:, 0:1], acctok[:, p2], axis=AX.X), VV)
        VV.wt('v')
        op('v', lambda v, p2=p2, st_=st_: v.reduce_sum(st_[:, 1:2], scrc[:, p2], axis=AX.X))
        op('v', lambda v, st_=st_: v.tensor_scalar_mul(st_[:, 0:1], st_[:, 0:1], 1.0 / D), VV)
        VV.wt('v')
        op('v', lambda v, st_=st_: v.tensor_scalar_mul(st_[:, 1:2], st_[:, 1:2], 1.0 / D))
        op('v', lambda v, st_=st_: v.tensor_mul(st_[:, 2:3], st_[:, 0:1], st_[:, 0:1]), VV)
        VV.wt('v')
        op('v', lambda v, st_=st_: v.tensor_sub(st_[:, 2:3], st_[:, 1:2], st_[:, 2:3]), VV)
        VV.wt('s')
        op('s', lambda s, st_=st_: s.activation(st_[:, 3:4], st_[:, 2:3], AF.Ln, bias=consts[:, 0:1], scale=1.0), SS)
        SS.wt('s')
        op('s', lambda s, st_=st_: s.activation(st_[:, 4:5], st_[:, 3:4], AF.Exp, scale=consts[:, 1:2]), SS)
        SS.wt('v')
        op('v', lambda v, st_=st_: v.tensor_scalar(st_[:, 5:6], st_[:, 0:1], st_[:, 4:5], -1.0,
                                                   ALU.mult, ALU.mult), VV)
        VV.wt('v')
        op('v', lambda v, p2=p2, st_=st_: v.tensor_scalar(
            tmpc[:, p2], acctok[:, p2], st_[:, 4:5], st_[:, 5:6], ALU.mult, ALU.add), VV)
        VV.wt('v')
        XCj = XCa if (t_i // 4) % 2 == 0 else XCb
        XCj.wt('v', 16 * ((t_i // 4) // 2 + 1))
        if t_i >= 2:
            SPT = STa if p2 == 0 else STb
            SPT.wt('v', tok_c_st[t_i - 2])
        st_tok = op('v', lambda v, t_i=t_i: v.tensor_add(
            outb[:, t_i % 2], xrel[:, (t_i // 4) % 2, t_i % 4], tmpc[:, t_i % 2]), VV)
        tok_c_add[t_i] = st_tok
        if t_i % 4 == 3 and t_i // 4 + 2 < NJ:
            g_jj = t_i // 4 + 2
            VV.wt('y', st_tok)
            op('y', lambda e, g_jj=g_jj: e.dma_start(
                xrel[:, g_jj % 2], xsp_r[:, g_jj]), XCa if g_jj % 2 == 0 else XCb, 16)
        VV.wt('y', st_tok)
        tok_c_st[t_i] = op('y', lambda e, t_i=t_i, p2=p2: e.dma_start(out_t[:, t_i], outb[:, p2]),
                           STa if p2 == 0 else STb, 16)
    STa.wt('y'); STb.wt('y')
    c2.close()
    acc_ctx.close()

    with nc.Block() as block:
        block.gpsimd(lambda g: [f(g) for f in steps['g']])
        block.vector(lambda v: [f(v) for f in steps['v']])
        block.tensor(lambda t: [f(t) for f in steps['t']])
        block.scalar(lambda s: [f(s) for f in steps['s']])
        block.sync(lambda sy: [f(sy) for f in steps['y']])
    ctx.close()
    return nc


_cache = {}


def kernel(L0_attn_x, L0_attn_W_qkv, L0_attn_b_qkv, L0_attn_W_out, L0_attn_b_out,
           L0_ffn_Wg, L0_ffn_Wu, L0_ffn_Wd):
    if 'nc' not in _cache:
        _cache['nc'] = build()
    nc = _cache['nc']
    x = np.asarray(L0_attn_x, dtype=np.float32).reshape(NCORE * T, D)
    wqkv = np.ascontiguousarray(np.asarray(L0_attn_W_qkv, dtype=np.float32))
    bqkv = np.ascontiguousarray(np.asarray(L0_attn_b_qkv, dtype=np.float32))
    wout = np.ascontiguousarray(np.asarray(L0_attn_W_out, dtype=np.float32))
    bout = np.ascontiguousarray(np.asarray(L0_attn_b_out, dtype=np.float32))
    wg = np.ascontiguousarray(np.asarray(L0_ffn_Wg, dtype=np.float32))
    wu = np.ascontiguousarray(np.asarray(L0_ffn_Wu, dtype=np.float32))
    wd = np.ascontiguousarray(np.asarray(L0_ffn_Wd, dtype=np.float32))
    in_maps = []
    for c in range(NCORE):
        in_maps.append({
            "x": np.ascontiguousarray(
                x[c * T:(c + 1) * T].reshape(NT, 128, D).swapaxes(0, 1)),
            "wqkv": wqkv, "bqkv": bqkv, "wout": wout, "bout": bout,
            "wg": wg, "wu": wu, "wd": wd,
        })
    res = run_bass_kernel_spmd(nc, in_maps, core_ids=list(range(NCORE)))
    out = np.concatenate(
        [r["out"].swapaxes(0, 1).reshape(T, D) for r in res.results], axis=0)
    return out.reshape(1, NCORE * T, D)


# version 11
# speedup vs baseline: 1.0425x; 1.0425x over previous
import numpy as np
from contextlib import ExitStack

import concourse.bass as bass
import concourse.mybir as mybir
from concourse.bass_utils import run_bass_kernel_spmd

F32 = mybir.dt.float32
BF16 = mybir.dt.bfloat16
AF = mybir.ActivationFunctionType
ALU = mybir.AluOpType
AX = mybir.AxisListType

NCORE = 8
T = 2048
NT = 16
NJ = 4
D = 1024
H3 = 3072
NE = 8
FF = 4096
G = 512
NG = FF // G
EPS = 1e-5
MAGIC = float(np.int32(0x5F3759DF).view(np.float32))


def build(mode='full', nt_lim=NT):
    nc = bass.Bass("TRN2", target_bir_lowering=False)
    x_d = nc.dram_tensor("x", [128, NT, D], F32, kind="ExternalInput").ap()
    wqkv_d = nc.dram_tensor("wqkv", [D, H3], F32, kind="ExternalInput").ap()
    bqkv_d = nc.dram_tensor("bqkv", [H3], F32, kind="ExternalInput").ap()
    wout_d = nc.dram_tensor("wout", [D, D], F32, kind="ExternalInput").ap()
    bout_d = nc.dram_tensor("bout", [D], F32, kind="ExternalInput").ap()
    wg_d = nc.dram_tensor("wg", [NE, D, FF], F32, kind="ExternalInput").ap()
    wu_d = nc.dram_tensor("wu", [NE, D, FF], F32, kind="ExternalInput").ap()
    wd_d = nc.dram_tensor("wd", [NE, FF, D], F32, kind="ExternalInput").ap()
    out_d = nc.dram_tensor("out", [128, NT, D], F32, kind="ExternalOutput").ap()
    xsp_d = nc.dram_tensor("xspill", [128, NT, D], F32, kind="Internal").ap()

    ctx = ExitStack()
    S = lambda n, sh, dt: ctx.enter_context(nc.sbuf_tensor(n, sh, dt))

    ident = S("ident", [128, 128], BF16)
    ones1 = S("ones1", [1, 128], BF16)
    consts = S("consts", [128, 3], F32)
    bq_sb = S("bq_sb", [1, H3], BF16)
    bo_sb = S("bo_sb", [1, D], BF16)
    ln2T = S("ln2T", [128, 8, T], BF16)
    stats = S("stats", [128, 2, 8], F32)
    scores = S("scores", [128, 256], F32)
    mx = S("mx", [128, 16], F32)
    sc2 = S("sc2", [128, 256], BF16)
    probs = S("probs", [128, 256], BF16)
    probsn = S("probsn", [128, 256], BF16)
    ssum = S("ssum", [128, 16], F32)
    rs = S("rs", [128, 16], F32)

    steps = {k: [] for k in "gvtsy"}

    def em(k, fn):
        steps[k].append(fn)

    class Ch:
        __slots__ = ("h", "v")

        def __init__(self, name):
            self.h = ctx.enter_context(nc.semaphore(name))
            self.v = 0

        def wt(self, k, tok=None):
            t_ = self.v if tok is None else tok
            em(k, lambda e, h=self.h, t=t_: e.wait_ge(h, t))

    def op(k, fn, ch=None, n=1):
        if ch is not None:
            ch.v += n
            em(k, lambda e, fn=fn, h=ch.h, n=n: fn(e).then_inc(h, n))
            return ch.v
        em(k, fn)
        return None

    XLa = Ch("XLa"); XLb = Ch("XLb")
    STa = Ch("STa"); STb = Ch("STb")
    STD = Ch("STD")
    XCa = Ch("XCa"); XCb = Ch("XCb")
    SP = Ch("SP")
    STp = [None, None]
    WIq = Ch("WIq"); WIo = Ch("WIo")
    WLa = Ch("WLa"); WLb = Ch("WLb")
    GI = Ch("GI")
    VV = Ch("VV")
    VE = Ch("VE")
    MUL = Ch("MUL"); ACC = Ch("ACC"); LNA = Ch("LNA")
    SS = Ch("SS")
    QE = Ch("QE")
    SIL = Ch("SIL")
    CS = Ch("CS")
    GP = Ch("GP")
    TB = Ch("TB")
    MM = Ch("MM")
    MMG = Ch("MMG"); MMU = Ch("MMU"); MMD = Ch("MMD")

    acc_ctx = ExitStack()
    actx = ExitStack()
    Sa = lambda n, sh, dt: actx.enter_context(nc.sbuf_tensor(n, sh, dt))
    Pa = lambda n, sh, dt: actx.enter_context(nc.psum_tensor(n, sh, dt))
    xf = Sa("xf", [128, 2, 4, D], F32)
    wqkv_sb = Sa("wqkv_sb", [128, 8, H3], BF16)
    wout_sb = Sa("wout_sb", [128, 8, D], BF16)
    ln1 = Sa("ln1", [128, 2, D], BF16)
    ln1T = Sa("ln1T", [128, 2, 8, 128], BF16)
    qkv = Sa("qkv", [128, 2, H3], BF16)
    prod = Sa("prod", [128, 2, 4096], BF16)
    ctxt = Sa("ctxt", [128, 2, D], BF16)
    ctxT = Sa("ctxT", [128, 2, 8, 128], BF16)
    ln2tok = Sa("ln2tok", [128, 2, D], BF16)
    scr = Sa("scr", [128, 2, D], F32)
    vtc = Sa("vtc", [128, 1024], BF16)
    tmpl = Sa("tmpl", [128, D], F32)
    ps_tr = Pa("ps_tr", [128, 2, 8, 128], BF16)
    ps_mm = Pa("ps_mm", [128, 2, 512], F32)
    ps_att = Pa("ps_att", [128, 2, 2, 512], F32)

    x_r = x_d.rearrange("p (j i) d -> p j i d", i=4)
    xsp_r = xsp_d.rearrange("p (j i) d -> p j i d", i=4)
    op('y', lambda e: e.dma_start(xf[:, 0], x_r[:, 0]), XLa, 16)
    op('y', lambda e: e.dma_start(xf[:, 1], x_r[:, 1]), XLb, 16)
    op('g', lambda g: g.dma_start(wqkv_sb[:], wqkv_d.rearrange("(k p) e -> p k e", p=128)), WIq, 16)
    op('g', lambda g: g.dma_start(bq_sb[:], bqkv_d[None, :]), WIq, 16)
    op('g', lambda g: g.dma_start(wout_sb[:], wout_d.rearrange("(k p) e -> p k e", p=128)), WIo, 16)
    op('g', lambda g: g.dma_start(bo_sb[:], bout_d[None, :]), WIo, 16)
    op('g', lambda g: g.memset(ident[:], 0.0), GI)
    GI.wt('g')
    op('g', lambda g: g.affine_select(out=ident[:], in_=ident[:], compare_op=ALU.not_equal,
                                      fill=1.0, base=0, pattern=[[-1, 128]], channel_multiplier=1), GI)
    op('v', lambda v: v.memset(ones1[:], 1.0))
    op('v', lambda v: v.memset(consts[:, 0:1], EPS))
    op('v', lambda v: v.memset(consts[:, 1:2], -0.5))
    op('v', lambda v: v.memset(consts[:, 2:3], MAGIC), VV)

    def ln_stats(src_fn, par):
        st_ = stats[:, par]
        op('v', lambda v: v.tensor_mul(scr[:, par], src_fn(), src_fn()))
        op('v', lambda v: v.reduce_sum(st_[:, 0:1], src_fn(), axis=AX.X), VV)
        VV.wt('v')
        op('v', lambda v: v.reduce_sum(st_[:, 1:2], scr[:, par], axis=AX.X))
        op('v', lambda v: v.tensor_scalar_mul(st_[:, 0:1], st_[:, 0:1], 1.0 / D), VV)
        VV.wt('v')
        op('v', lambda v: v.tensor_scalar_mul(st_[:, 1:2], st_[:, 1:2], 1.0 / D))
        op('v', lambda v: v.tensor_mul(st_[:, 2:3], st_[:, 0:1], st_[:, 0:1]), VV)
        VV.wt('v')
        op('v', lambda v: v.tensor_sub(st_[:, 2:3], st_[:, 1:2], st_[:, 2:3]), VV)
        VV.wt('s')
        op('s', lambda s: s.activation(st_[:, 3:4], st_[:, 2:3], AF.Ln, bias=consts[:, 0:1], scale=1.0), SS)
        SS.wt('s')
        op('s', lambda s: s.activation(st_[:, 4:5], st_[:, 3:4], AF.Exp, scale=consts[:, 1:2]), SS)
        SS.wt('v')
        return op('v', lambda v: v.tensor_scalar(st_[:, 5:6], st_[:, 0:1], st_[:, 4:5], -1.0,
                                                 ALU.mult, ALU.mult), VV)

    def ln_stats_psum(src_fn, par):
        st_ = stats[:, par]
        op('v', lambda v: v.reduce_sum(st_[:, 0:1], src_fn(), axis=AX.X), VV)
        VV.wt('s')
        op('s', lambda s: s.activation(scr[:, par], src_fn(), AF.Square), SS)
        SS.wt('v')
        op('v', lambda v: v.reduce_sum(st_[:, 1:2], scr[:, par], axis=AX.X))
        op('v', lambda v: v.tensor_scalar_mul(st_[:, 0:1], st_[:, 0:1], 1.0 / D), VV)
        VV.wt('v')
        op('v', lambda v: v.tensor_scalar_mul(st_[:, 1:2], st_[:, 1:2], 1.0 / D))
        op('v', lambda v: v.tensor_mul(st_[:, 2:3], st_[:, 0:1], st_[:, 0:1]), VV)
        VV.wt('v')
        op('v', lambda v: v.tensor_sub(st_[:, 2:3], st_[:, 1:2], st_[:, 2:3]), VV)
        VV.wt('s')
        op('s', lambda s: s.activation(st_[:, 3:4], st_[:, 2:3], AF.Ln, bias=consts[:, 0:1], scale=1.0), SS)
        SS.wt('s')
        op('s', lambda s: s.activation(st_[:, 4:5], st_[:, 3:4], AF.Exp, scale=consts[:, 1:2]), SS)
        SS.wt('v')
        return op('v', lambda v: v.tensor_scalar(st_[:, 5:6], st_[:, 0:1], st_[:, 4:5], -1.0,
                                                 ALU.mult, ALU.mult), VV)

    tok_tr_evac = {}
    tok_qkv_mm = {}
    tok_wout_mm = {}
    tok_qkv_evac = {}
    tok_hm_done = {}
    tok_ln1_tr = {}
    tok_ctx_tr = {}
    tok_lna = {}
    tok_ln2_ev = {}
    tok_prod = {}
    tok_ln1_w = {}
    tok_ln1T_ev = {}
    tok_qkv_ev_last = {}
    n_trb = [0]
    n_qg = [0]
    n_hg = [0]

    def transpose_batch(src_fn, dst):
        c = n_trb[0]; n_trb[0] += 1
        s_ = c % 2
        if c - 2 in tok_tr_evac:
            QE.wt('t', tok_tr_evac[c - 2])
        tr_tok = None
        for k in range(8):
            tr_tok = op('t', lambda t, k=k, s_=s_: t.transpose(ps_tr[:, s_, k], src_fn(k), ident[:]),
                        TB if k == 7 else None)
        TB.wt('s', tr_tok)
        ev = op('s', lambda s, s_=s_, dst=dst: s.activation(dst, ps_tr[:, s_], AF.Copy), QE)
        tok_tr_evac[c] = ev
        return tr_tok, ev

    qh = qkv
    ph = probsn

    def S1(t_i):
        j, i = divmod(t_i, 4)
        jp = j % 2
        p2 = t_i % 2
        xt = xf[:, jp, i]
        if t_i == 0:
            GI.wt('v'); VV.wt('v')
        if i == 0:
            XJ = XLa if j % 2 == 0 else XLb
            XJ.wt('v', 16 * (j // 2 + 1))
        ln_stats(lambda xt=xt: xt, p2)
        VV.wt('v')
        if t_i >= 2:
            TB.wt('v', tok_ln1_tr[t_i - 2])
        tok_ln1_w[t_i] = op('v', lambda v, xt=xt, p2=p2: v.tensor_scalar(
            ln1[:, p2], xt, stats[:, p2, 4:5], stats[:, p2, 5:6], ALU.mult, ALU.add), VE)

    def S2(t_i):
        p2 = t_i % 2
        VE.wt('t', tok_ln1_w[t_i])
        if t_i == 0:
            GI.wt('t')
        if t_i >= 2:
            MM.wt('v', tok_qkv_mm[t_i - 2])
        trt, ev1 = transpose_batch(lambda k, p2=p2: ln1[:, p2, k * 128:(k + 1) * 128], ln1T[:, p2])
        tok_ln1_tr[t_i] = trt
        tok_ln1T_ev[t_i] = ev1

    def S3(t_i):
        p2 = t_i % 2
        QE.wt('t', tok_ln1T_ev[t_i])
        if t_i == 0:
            WIq.wt('t')
        if t_i >= 2:
            VV.wt('s', tok_hm_done[t_i - 2])
        for n in range(6):
            b = n % 2
            if n_qg[0] - 2 in tok_qkv_evac:
                QE.wt('t', tok_qkv_evac[n_qg[0] - 2])
            op('t', lambda t, n=n, b=b: t.matmul(ps_mm[:, b], ones1[:],
                                                 bq_sb[:, n * 512:(n + 1) * 512], start=True, stop=False))
            mt = None
            for k in range(8):
                mt = op('t', lambda t, n=n, b=b, k=k, p2=p2: t.matmul(
                    ps_mm[:, b], ln1T[:, p2, k], wqkv_sb[:, k, n * 512:(n + 1) * 512],
                    start=False, stop=(k == 7)), MM if k == 7 else None)
            MM.wt('s', mt)
            tok_qkv_evac[n_qg[0]] = op(
                's', lambda s, n=n, b=b, p2=p2: s.activation(
                    qkv[:, p2, n * 512:(n + 1) * 512], ps_mm[:, b], AF.Copy), QE)
            n_qg[0] += 1
        tok_qkv_mm[t_i] = MM.v
        tok_qkv_ev_last[t_i] = QE.v

    def S4(t_i):
        p2 = t_i % 2
        QE.wt('v', tok_qkv_ev_last[t_i])
        qb = 3072 * p2
        op('v', lambda v, qb=qb: v.tensor_copy(
            vtc[:].rearrange("p (d g) -> p d g", g=16),
            bass.AP(qh, qb + 128, [[2 * H3, 128], [1, 64], [192, 16]])), VV)
        for grp in range(4):
            h0 = grp * 4
            gg = n_hg[0]; n_hg[0] += 1
            sl = gg % 2
            if gg - 2 in tok_prod:
                VV.wt('v', tok_prod[gg - 2])
            qap = bass.AP(qh, qb + 192 * h0, [[2 * H3, 128], [192, 4], [0, 16], [1, 64]])
            kap = bass.AP(qh, qb + 64, [[2 * H3, 128], [0, 4], [192, 16], [1, 64]])
            op('v', lambda v, sl=sl, qap=qap, kap=kap: v.tensor_tensor(
                prod[:, sl].rearrange("p (h g d) -> p h g d", h=4, g=16), qap, kap, ALU.mult), VV)
            VV.wt('v')
            for half in (32, 16, 8):
                op('v', lambda v, sl=sl, half=half: v.tensor_add(
                    bass.AP(prod, sl * 4096, [[8192, 128], [1024, 4], [64, 16], [1, half]]),
                    bass.AP(prod, sl * 4096, [[8192, 128], [1024, 4], [64, 16], [1, half]]),
                    bass.AP(prod, sl * 4096 + half, [[8192, 128], [1024, 4], [64, 16], [1, half]])), VV)
                VV.wt('v')
            tok_prod[gg] = op('v', lambda v, grp=grp, sl=sl: v.reduce_sum(
                scores[:, grp * 64:(grp + 1) * 64].rearrange("p (h g) -> p h g", h=4),
                bass.AP(prod, sl * 4096, [[8192, 128], [1024, 4], [64, 16], [1, 8]]),
                axis=AX.X), VV)
        VV.wt('v')
        op('v', lambda v: v.tensor_scalar_mul(scores[:], scores[:], 0.125), VV)
        VV.wt('v')
        op('v', lambda v: v.reduce_max(mx[:], scores[:].rearrange("p (h g) -> p h g", h=16), axis=AX.X), VV)
        VV.wt('v')
        op('v', lambda v: v.tensor_sub(sc2[:].rearrange("p (h g) -> p h g", h=16),
                                       scores[:].rearrange("p (h g) -> p h g", h=16),
                                       mx[:].unsqueeze(2).broadcast_to([128, 16, 16])), VV)
        VV.wt('s')
        op('s', lambda s: s.activation(probs[:], sc2[:], AF.Exp), SS)
        SS.wt('v')
        op('v', lambda v: v.reduce_sum(ssum[:], probs[:].rearrange("p (h g) -> p h g", h=16), axis=AX.X), VV)
        VV.wt('v')
        op('v', lambda v: v.reciprocal(rs[:], ssum[:]), VV)
        VV.wt('v')
        op('v', lambda v: v.tensor_tensor(probsn[:].rearrange("p (h g) -> p h g", h=16),
                                          probs[:].rearrange("p (h g) -> p h g", h=16),
                                          rs[:].unsqueeze(2).broadcast_to([128, 16, 16]), ALU.mult), VV)
        VV.wt('v')
        if t_i >= 2:
            TB.wt('v', tok_ctx_tr[t_i - 2])
        for grp in range(4):
            h0 = grp * 4
            gg = n_hg[0]; n_hg[0] += 1
            sl = gg % 2
            if gg - 2 in tok_prod:
                VV.wt('v', tok_prod[gg - 2])
            pap = bass.AP(ph, 16 * h0, [[256, 128], [16, 4], [0, 64], [1, 16]])
            vap = bass.AP(vtc, 0, [[1024, 128], [0, 4], [16, 64], [1, 16]])
            op('v', lambda v, sl=sl, pap=pap, vap=vap: v.tensor_tensor(
                prod[:, sl].rearrange("p (h d g) -> p h d g", h=4, d=64), pap, vap, ALU.mult), VV)
            VV.wt('v')
            for half in (8, 4, 2):
                op('v', lambda v, sl=sl, half=half: v.tensor_add(
                    bass.AP(prod, sl * 4096, [[8192, 128], [1024, 4], [16, 64], [1, half]]),
                    bass.AP(prod, sl * 4096, [[8192, 128], [1024, 4], [16, 64], [1, half]]),
                    bass.AP(prod, sl * 4096 + half, [[8192, 128], [1024, 4], [16, 64], [1, half]])), VV)
                VV.wt('v')

            def cxred(v, grp=grp, sl=sl, p2=p2):
                with nc.allow_low_precision(reason="context feeds a bf16 matmul"):
                    return v.tensor_add(
                        ctxt[:, p2, grp * 256:(grp + 1) * 256].rearrange("p (h d) -> p h d", h=4),
                        bass.AP(prod, sl * 4096, [[8192, 128], [1024, 4], [16, 64], [1, 1]]),
                        bass.AP(prod, sl * 4096 + 1, [[8192, 128], [1024, 4], [16, 64], [1, 1]]))
            tok_prod[gg] = op('v', cxred, VV)
        tok_hm_done[t_i] = VV.v

    def S5(t_i):
        p2 = t_i % 2
        VV.wt('t', tok_hm_done[t_i])
        if t_i >= 2:
            MM.wt('v', tok_wout_mm[t_i - 2])
        trt2, ev2 = transpose_batch(lambda k, p2=p2: ctxt[:, p2, k * 128:(k + 1) * 128], ctxT[:, p2])
        tok_ctx_tr[t_i] = trt2
        QE.wt('t', ev2)
        if t_i == 0:
            WIo.wt('t')
        if t_i >= 2:
            LNA.wt('t', tok_lna[t_i - 2])
        mt = None
        for n in range(2):
            op('t', lambda t, n=n, p2=p2: t.matmul(ps_att[:, p2, n], ones1[:],
                                                   bo_sb[:, n * 512:(n + 1) * 512], start=True, stop=False))
            for k in range(8):
                mt = op('t', lambda t, n=n, k=k, p2=p2: t.matmul(
                    ps_att[:, p2, n], ctxT[:, p2, k], wout_sb[:, k, n * 512:(n + 1) * 512],
                    start=False, stop=(k == 7)), MM if k == 7 else None)
        tok_wout_mm[t_i] = mt

    def S6(t_i):
        j, i = divmod(t_i, 4)
        jp = j % 2
        p2 = t_i % 2
        xt = xf[:, jp, i]
        mt = tok_wout_mm[t_i]
        MM.wt('v', mt)
        MM.wt('s', mt)
        psa = lambda p2=p2: ps_att[:, p2].rearrange("p a b -> p (a b)")
        ln_stats_psum(psa, p2)
        VV.wt('v')
        op('v', lambda v, p2=p2, psa=psa: v.tensor_scalar(
            tmpl[:], psa(), stats[:, p2, 4:5], stats[:, p2, 5:6], ALU.mult, ALU.add), LNA)
        tok_lna[t_i] = LNA.v
        LNA.wt('v', tok_lna[t_i])
        op('v', lambda v, xt=xt: v.tensor_add(xt, xt, tmpl[:]), VV)
        VV.wt('v')

    def S7(t_i):
        j, i = divmod(t_i, 4)
        jp = j % 2
        p2 = t_i % 2
        xt = xf[:, jp, i]
        ln_stats(lambda xt=xt: xt, p2)
        VV.wt('v')
        if t_i >= 2:
            QE.wt('v', tok_ln2_ev[t_i - 2])
        w_ln2 = op('v', lambda v, xt=xt, p2=p2: v.tensor_scalar(
            ln2tok[:, p2], xt, stats[:, p2, 4:5], stats[:, p2, 5:6], ALU.mult, ALU.add), VE)
        VE.wt('t', w_ln2)
        _, ev = transpose_batch(lambda k, p2=p2: ln2tok[:, p2, k * 128:(k + 1) * 128],
                                ln2T[:, :, t_i * 128:(t_i + 1) * 128])
        tok_ln2_ev[t_i] = ev
        if i == 3:
            QE.wt('y', ev)
            op('y', lambda e, jp=jp, j=j: e.dma_start(xsp_r[:, j], xf[:, jp]), SP, 16)
            if j + 2 < NJ:
                SP.wt('y')
                op('y', lambda e, j=j, jp=jp: e.dma_start(xf[:, jp], x_r[:, j + 2]),
                   XLa if (j + 2) % 2 == 0 else XLb, 16)

    S1(0)
    for t_i in range(nt_lim):
        if t_i >= 1:
            S6(t_i - 1)
            S7(t_i - 1)
        S2(t_i)
        S3(t_i)
        if t_i + 1 < nt_lim:
            S1(t_i + 1)
        S4(t_i)
        S5(t_i)
    S6(nt_lim - 1)
    S7(nt_lim - 1)

    AT_tok = TB.v
    AV_tok = VE.v
    AVV_tok = VV.v
    AS_tok = SS.v
    AQ_tok = QE.v
    actx.close()

    if mode == 'A':
        SP.wt('y')
        op('y', lambda e: e.dma_start(out_d, xsp_d), STD, 16)
        STD.wt('y')
        with nc.Block() as block:
            block.gpsimd(lambda g: [f(g) for f in steps['g']])
            block.vector(lambda v: [f(v) for f in steps['v']])
            block.tensor(lambda t: [f(t) for f in steps['t']])
            block.scalar(lambda s: [f(s) for f in steps['s']])
            block.sync(lambda sy: [f(sy) for f in steps['y']])
        ctx.close()
        return nc

    acc = acc_ctx.enter_context(nc.sbuf_tensor("acc", [128, 8, T], F32))
    acc_bf = acc_ctx.enter_context(nc.sbuf_tensor("acc_bf", [128, 8, T], BF16))
    bctx = ExitStack()
    Sb = lambda n, sh, dt: bctx.enter_context(nc.sbuf_tensor(n, sh, dt))
    Pb = lambda n, sh, dt: bctx.enter_context(nc.psum_tensor(n, sh, dt))
    wg_b = Sb("wg_b", [128, 2, 8, G], BF16)
    wu_b = Sb("wu_b", [128, 2, 8, G], BF16)
    wd_b = Sb("wd_b", [128, 2, 4, D], BF16)
    hbuf = Sb("hbuf", [128, 2, 4, 512], BF16)
    silb = Sb("silb", [128, 2, 512], BF16)
    ps_g = Pb("ps_g", [128, 2, 512], F32)
    ps_u = Pb("ps_u", [128, 2, 512], F32)
    ps_d = Pb("ps_d", [128, 4, 512], F32)


    WL = [WLa, WLb]
    wl_cnt = [0, 0]
    mmd_gp = {}
    silu_tok = {}
    mul_tok = {}
    accw_tok = {}
    spill_done = 16 * NJ
    cglob = [0]

    NQ = NE * NG * NJ
    mmd_q = {}

    def emit_weights(gp):
        par = gp % 2
        if gp == 0:
            TB.wt('g', AT_tok); VE.wt('g', AV_tok); VV.wt('g', AVV_tok)
            SS.wt('g', AS_tok); QE.wt('g', AQ_tok)
        if gp >= 2:
            MMD.wt('g', mmd_gp[gp - 2])
        e, gi = divmod(gp, NG)
        op('g', lambda g, e=e, gi=gi, par=par: g.dma_start(
            wg_b[:, par], wg_d[e].rearrange("(k p) f -> p k f", p=128)[:, :, gi * G:(gi + 1) * G]),
            WL[par], 16)
        op('g', lambda g, e=e, gi=gi, par=par: g.dma_start(
            wu_b[:, par], wu_d[e].rearrange("(k p) f -> p k f", p=128)[:, :, gi * G:(gi + 1) * G]),
            WL[par], 16)
        op('g', lambda g, e=e, gi=gi, par=par: g.dma_start(
            wd_b[:, par], wd_d[e].rearrange("(k p) f -> p k f", p=128)[:, gi * 4:(gi + 1) * 4, :]),
            WL[par], 16)
        wl_cnt[par] += 48

    def emit_P1(q):
        gp, jj = divmod(q, NJ)
        par = gp % 2
        toks = slice(jj * 512, (jj + 1) * 512)
        if jj == 0:
            emit_weights(gp)
            WL[par].wt('t', wl_cnt[par])
        for kh in range(4):
            c = cglob[0]; cglob[0] += 1
            bk = c % 2
            if c - 2 in silu_tok:
                SIL.wt('t', silu_tok[c - 2])
                MUL.wt('t', mul_tok[c - 2])
            if kh == 0 and q - 2 in mmd_q:
                MMD.wt('v', mmd_q[q - 2])
            gt = None
            for kd in range(8):
                gt = op('t', lambda t, par=par, kh=kh, kd=kd, bk=bk, toks=toks: t.matmul(
                    ps_g[:, bk], wg_b[:, par, kd, kh * 128:(kh + 1) * 128], ln2T[:, kd, toks],
                    start=(kd == 0), stop=(kd == 7)), MMG if kd == 7 else None)
            ut = None
            for kd in range(8):
                ut = op('t', lambda t, par=par, kh=kh, kd=kd, bk=bk, toks=toks: t.matmul(
                    ps_u[:, bk], wu_b[:, par, kd, kh * 128:(kh + 1) * 128], ln2T[:, kd, toks],
                    start=(kd == 0), stop=(kd == 7)), MMU if kd == 7 else None)
            MMG.wt('s', gt)
            if c - 2 in mul_tok:
                MUL.wt('s', mul_tok[c - 2])
            silu_tok[c] = op('s', lambda s, bk=bk, c=c: s.activation(
                silb[:, c % 2], ps_g[:, bk], AF.Silu), SIL)
            MMU.wt('v', ut)
            SIL.wt('v', silu_tok[c])
            mul_tok[c] = op('v', lambda v, kh=kh, bk=bk, c=c, q=q: v.tensor_tensor(
                hbuf[:, q % 2, kh], silb[:, c % 2], ps_u[:, bk], ALU.mult), MUL)

    def emit_P2(q):
        gp, jj = divmod(q, NJ)
        par = gp % 2
        toks = slice(jj * 512, (jj + 1) * 512)
        MUL.wt('t', mul_tok[q * 4 + 3])
        for half in range(2):
            for mi in range(4):
                m = half * 4 + mi
                pk = (q, 0, mi) if half == 1 else (q - 1, 1, mi)
                if pk in accw_tok:
                    ACC.wt('t', accw_tok[pk])
                dt_ = None
                for kh in range(4):
                    dt_ = op('t', lambda t, par=par, m=m, mi=mi, kh=kh, q=q: t.matmul(
                        ps_d[:, mi], wd_b[:, par, kh, m * 128:(m + 1) * 128], hbuf[:, q % 2, kh],
                        start=(kh == 0), stop=(kh == 3)), MMD if kh == 3 else None)
                MMD.wt('v', dt_)
                if gp == 0:
                    if q == 0 and half == 0 and mi == 0:
                        SP.wt('v', spill_done)
                    tok = op('v', lambda v, m=m, mi=mi, toks=toks: v.tensor_copy(
                        acc[:, m, toks], ps_d[:, mi]), ACC)
                else:
                    tok = op('v', lambda v, m=m, mi=mi, toks=toks: v.tensor_add(
                        acc[:, m, toks], ps_d[:, mi], acc[:, m, toks]), ACC)
                accw_tok[(q, half, mi)] = tok
        mmd_q[q] = MMD.v
        if jj == NJ - 1:
            mmd_gp[gp] = MMD.v

    for q in range(NQ):
        emit_P1(q)
        if q >= 1:
            emit_P2(q - 1)
    emit_P2(NQ - 1)

    B_acc_done = ACC.v
    bctx.close()

    if mode == 'AB':
        ACC.wt('g')
        op('g', lambda g: g.dma_start(out_d.rearrange("(k p) tt -> p k tt", p=128), acc[:]), STD, 16)
        STD.wt('g')
        with nc.Block() as block:
            block.gpsimd(lambda g: [f(g) for f in steps['g']])
            block.vector(lambda v: [f(v) for f in steps['v']])
            block.tensor(lambda t: [f(t) for f in steps['t']])
            block.scalar(lambda s: [f(s) for f in steps['s']])
            block.sync(lambda sy: [f(sy) for f in steps['y']])
        ctx.close()
        acc_ctx.close()
        return nc

    c2 = ExitStack()
    Sc = lambda n, sh, dt: c2.enter_context(nc.sbuf_tensor(n, sh, dt))
    Pc = lambda n, sh, dt: c2.enter_context(nc.psum_tensor(n, sh, dt))
    acctok = Sc("acctok", [128, 2, D], BF16)
    xrel = Sc("xrel", [128, 2, 4, D], F32)
    tmpc = Sc("tmpc", [128, 2, D], F32)
    scrc = Sc("scrc", [128, 2, D], F32)
    outb = Sc("outb", [128, 2, D], F32)
    ps_c = Pc("ps_c", [128, 2, 8, 128], BF16)

    conv_tok = {}
    for t_c in range(NT):
        g_jj = t_c // 4
        if t_c % 4 == 0:
            ACC.wt('s', accw_tok[((NE * NG - 1) * NJ + g_jj, 1, 3)])
        conv_tok[t_c] = op('s', lambda s, t_c=t_c: s.activation(
            acc_bf[:, :, t_c * 128:(t_c + 1) * 128],
            acc[:, :, t_c * 128:(t_c + 1) * 128], AF.Copy), CS)

    xsp_t = xsp_d
    out_t = out_d
    tok_c_ev = {}
    tok_c_st = {}
    MMD.wt('y'); WLa.wt('y'); WLb.wt('y')
    for g_jj in range(2):
        op('y', lambda e, g_jj=g_jj: e.dma_start(
            xrel[:, g_jj % 2], xsp_r[:, g_jj]), XCa if g_jj % 2 == 0 else XCb, 16)
    tok_c_add = {}
    for t_i in range(NT):
        p2 = t_i % 2
        if t_i == 0:
            ACC.wt('t', B_acc_done)
        CS.wt('t', conv_tok[t_i])
        if t_i >= 2:
            VE.wt('t', tok_c_ev[t_i - 2])
        ct = None
        for k in range(8):
            ct = op('t', lambda t, k=k, t_i=t_i, p2=p2: t.transpose(
                ps_c[:, p2, k], acc_bf[:, k, t_i * 128:(t_i + 1) * 128], ident[:]),
                TB if k == 7 else None)
        TB.wt('v', ct)
        tok_c_ev[t_i] = op('v', lambda v, p2=p2: v.tensor_copy(acctok[:, p2], ps_c[:, p2]), VE)
        VE.wt('v', tok_c_ev[t_i])
        st_ = stats[:, p2]
        op('v', lambda v, p2=p2: v.tensor_mul(scrc[:, p2], acctok[:, p2], acctok[:, p2]))
        op('v', lambda v, p2=p2, st_=st_: v.reduce_sum(st_[:, 0:1], acctok[:, p2], axis=AX.X), VV)
        VV.wt('v')
        op('v', lambda v, p2=p2, st_=st_: v.reduce_sum(st_[:, 1:2], scrc[:, p2], axis=AX.X))
        op('v', lambda v, st_=st_: v.tensor_scalar_mul(st_[:, 0:1], st_[:, 0:1], 1.0 / D), VV)
        VV.wt('v')
        op('v', lambda v, st_=st_: v.tensor_scalar_mul(st_[:, 1:2], st_[:, 1:2], 1.0 / D))
        op('v', lambda v, st_=st_: v.tensor_mul(st_[:, 2:3], st_[:, 0:1], st_[:, 0:1]), VV)
        VV.wt('v')
        op('v', lambda v, st_=st_: v.tensor_sub(st_[:, 2:3], st_[:, 1:2], st_[:, 2:3]), VV)
        VV.wt('s')
        op('s', lambda s, st_=st_: s.activation(st_[:, 3:4], st_[:, 2:3], AF.Ln, bias=consts[:, 0:1], scale=1.0), SS)
        SS.wt('s')
        op('s', lambda s, st_=st_: s.activation(st_[:, 4:5], st_[:, 3:4], AF.Exp, scale=consts[:, 1:2]), SS)
        SS.wt('v')
        op('v', lambda v, st_=st_: v.tensor_scalar(st_[:, 5:6], st_[:, 0:1], st_[:, 4:5], -1.0,
                                                   ALU.mult, ALU.mult), VV)
        VV.wt('v')
        op('v', lambda v, p2=p2, st_=st_: v.tensor_scalar(
            tmpc[:, p2], acctok[:, p2], st_[:, 4:5], st_[:, 5:6], ALU.mult, ALU.add), VV)
        VV.wt('v')
        XCj = XCa if (t_i // 4) % 2 == 0 else XCb
        XCj.wt('v', 16 * ((t_i // 4) // 2 + 1))
        if t_i >= 2:
            SPT = STa if p2 == 0 else STb
            SPT.wt('v', tok_c_st[t_i - 2])
        st_tok = op('v', lambda v, t_i=t_i: v.tensor_add(
            outb[:, t_i % 2], xrel[:, (t_i // 4) % 2, t_i % 4], tmpc[:, t_i % 2]), VV)
        tok_c_add[t_i] = st_tok
        if t_i % 4 == 3 and t_i // 4 + 2 < NJ:
            g_jj = t_i // 4 + 2
            VV.wt('y', st_tok)
            op('y', lambda e, g_jj=g_jj: e.dma_start(
                xrel[:, g_jj % 2], xsp_r[:, g_jj]), XCa if g_jj % 2 == 0 else XCb, 16)
        VV.wt('y', st_tok)
        tok_c_st[t_i] = op('y', lambda e, t_i=t_i, p2=p2: e.dma_start(out_t[:, t_i], outb[:, p2]),
                           STa if p2 == 0 else STb, 16)
    STa.wt('y'); STb.wt('y')
    c2.close()
    acc_ctx.close()

    with nc.Block() as block:
        block.gpsimd(lambda g: [f(g) for f in steps['g']])
        block.vector(lambda v: [f(v) for f in steps['v']])
        block.tensor(lambda t: [f(t) for f in steps['t']])
        block.scalar(lambda s: [f(s) for f in steps['s']])
        block.sync(lambda sy: [f(sy) for f in steps['y']])
    ctx.close()
    return nc


_cache = {}


def kernel(L0_attn_x, L0_attn_W_qkv, L0_attn_b_qkv, L0_attn_W_out, L0_attn_b_out,
           L0_ffn_Wg, L0_ffn_Wu, L0_ffn_Wd):
    if 'nc' not in _cache:
        _cache['nc'] = build()
    nc = _cache['nc']
    x = np.asarray(L0_attn_x, dtype=np.float32).reshape(NCORE * T, D)
    wqkv = np.ascontiguousarray(np.asarray(L0_attn_W_qkv, dtype=np.float32))
    bqkv = np.ascontiguousarray(np.asarray(L0_attn_b_qkv, dtype=np.float32))
    wout = np.ascontiguousarray(np.asarray(L0_attn_W_out, dtype=np.float32))
    bout = np.ascontiguousarray(np.asarray(L0_attn_b_out, dtype=np.float32))
    wg = np.ascontiguousarray(np.asarray(L0_ffn_Wg, dtype=np.float32))
    wu = np.ascontiguousarray(np.asarray(L0_ffn_Wu, dtype=np.float32))
    wd = np.ascontiguousarray(np.asarray(L0_ffn_Wd, dtype=np.float32))
    in_maps = []
    for c in range(NCORE):
        in_maps.append({
            "x": np.ascontiguousarray(
                x[c * T:(c + 1) * T].reshape(NT, 128, D).swapaxes(0, 1)),
            "wqkv": wqkv, "bqkv": bqkv, "wout": wout, "bout": bout,
            "wg": wg, "wu": wu, "wd": wd,
        })
    res = run_bass_kernel_spmd(nc, in_maps, core_ids=list(range(NCORE)))
    out = np.concatenate(
        [r["out"].swapaxes(0, 1).reshape(T, D) for r in res.results], axis=0)
    return out.reshape(1, NCORE * T, D)
